# version 01
# baseline (speedup 1.0000x reference)
import numpy as np
import ml_dtypes
from contextlib import ExitStack

import concourse.bass as bass
import concourse.mybir as mybir
from concourse.bass_utils import run_bass_kernel_spmd

F32 = mybir.dt.float32
BF16 = mybir.dt.bfloat16
AF = mybir.ActivationFunctionType
ALU = mybir.AluOpType
bf16 = ml_dtypes.bfloat16

D = 2048
HD = 128
HQ = 16
HK = 8
CONV = 4
C = 64
NCH_Q = 4
QT = 256
NQ = 8
W_CH = NCH_Q * 16
T_HALF = 2048
NITER = 29
RIDGE = 0.02
EPS = 1e-6
LN_EPS = 1e-5
E_EXP = 8
R_LORA = 16
SQ = float(HD) ** -0.5
KT = 16
KV_OT = 8

COEF_A = 0
COEF_B = NITER * 16
COEF_TH = 2 * NITER * 16
COEF_DG = 2 * NITER * 16 + 16
COEF_W = 2 * NITER * 16 + 32

DEBUG_TAPS = {}
PHASE_LIMIT = 99
NQLIM = 99
FINISH_SRC = "x"
NTAP = 1
FPARTS = 99
LNPARTS = 99
LNMM = 1
QPARTS = 99
PCPARTS = 99
ENGINES = ("sync", "tensor", "scalar", "vector", "gpsimd")


class Prog:

    def __init__(self, ename, eng, t, ctrs, taps):
        self.e = ename
        self.eng = eng
        self.t = t
        self.c = ctrs
        self.taps = taps

    def fire(self, ins, name, amt=1):
        c = self.c[name]
        c.v += amt
        if ins is not None:
            ins.then_inc(c.sem, amt)

    def w(self, who, name, val=None):
        if self.e == who:
            c = self.c[name]
            self.eng.wait_ge(c.sem, c.v if val is None else val)

    def on(self, who):
        return self.e == who

    def S(self, fn):
        return fn(self.eng) if self.e == "sync" else None

    def G(self, fn):
        return fn(self.eng) if self.e == "gpsimd" else None

    def P(self, fn):
        return fn(self.eng) if self.e == "tensor" else None

    def A(self, fn):
        return fn(self.eng) if self.e == "scalar" else None

    def V(self, fn):
        return fn(self.eng) if self.e == "vector" else None

    def tap(self, name, src_ap_fn):
        if name in self.taps:
            self.w("gpsimd", "dv")
            self.w("gpsimd", "ac")
            self.w("gpsimd", "pe")
            self.fire(self.G(lambda e: e.dma_start(out=self.taps[name], in_=src_ap_fn())), "sp", 16)

    def run(self):
        t = self.t
        self.setup()
        done = False
        if PHASE_LIMIT >= 1 and not done:
            for q in range(min(NQ, NQLIM)):
                self.quarter(t["xp_t"], q, main=False)
        if PHASE_LIMIT >= 2:
            self.mask_states()
        if PHASE_LIMIT >= 3:
            for q in range(min(NQ, PHASE_LIMIT - 2)):
                self.quarter(t["x_t"], q, main=True)
        if PHASE_LIMIT >= 11:
            self.ffn()
        else:
            self.finish_early()

    def finish_early(self):
        t = self.t
        for who in ENGINES:
            for cn in ("pe", "ac", "dv", "po", "sp", "sx", "cp0", "cp1", "ca0", "ca1", "cv0", "cv1"):
                self.w(who, cn)
        for dt in range(KT):
            self.fire(self.G(lambda e, dt=dt: e.dma_start(
                out=t["out_t"][dt * 128:(dt + 1) * 128, :],
                in_=(t["x1sp"][dt * 128:(dt + 1) * 128, :] if FINISH_SRC == "x1sp" else
                     t["x_t"][dt * 128:(dt + 1) * 128, 3:3 + T_HALF]))), "so", 16)
        self.w("gpsimd", "so")

    def setup(self):
        t = self.t
        for apn, tn in (("pmask", "pm_col"), ("nwv", "nw_col"), ("svec", "sv_col")):
            self.fire(self.S(lambda e, a=t[apn], b=t[tn]: e.dma_start(out=b[:], in_=a)), "sx", 16)
        for apn, tn in (("convq", "cvq_c"), ("convk", "cvk_c"), ("convv", "cvv_c")):
            self.fire(self.S(lambda e, a=t[apn], b=t[tn]: e.dma_start(out=b[:], in_=a)), "sx", 16)
        for hh in range(2):
            self.fire(self.S(lambda e, hh=hh: e.dma_start(
                out=t["wst_f"][hh][:], in_=t["wsm_t"][0][:, hh * 1024:(hh + 1) * 1024])), "sx", 16)
        G, P, A, V = self.G, self.P, self.A, self.V
        G(lambda e: e.memset(t["ident_b"][:], 0.0))
        G(lambda e: e.affine_select(out=t["ident_b"][:], in_=t["ident_b"][:], compare_op=ALU.not_equal,
                                    fill=1.0, base=0, pattern=[[-1, 128]], channel_multiplier=1))
        G(lambda e: e.memset(t["ident_f"][:], 0.0))
        G(lambda e: e.affine_select(out=t["ident_f"][:], in_=t["ident_f"][:], compare_op=ALU.not_equal,
                                    fill=1.0, base=0, pattern=[[-1, 128]], channel_multiplier=1))
        G(lambda e: e.memset(t["mask64"][:], 0.0))
        G(lambda e: e.affine_select(out=t["mask64"][:], in_=t["mask64"][:], compare_op=ALU.is_ge,
                                    fill=-100.0, base=0, pattern=[[1, 64]], channel_multiplier=-1))
        G(lambda e: e.memset(t["ones_f"][:], 1.0))
        G(lambda e: e.memset(t["ones_c"][:], 1.0))
        G(lambda e: e.memset(t["ones64"][:], 1.0))
        for j, v in ((0, LN_EPS), (1, 1.0), (2, EPS), (3, 1.0 / D), (4, -1.0), (5, 1.0 / HD)):
            G(lambda e, j=j, v=v: e.memset(t["kcc"][:, j:j + 1], v))
        self.w("gpsimd", "sx")
        for hh in range(2):
            ins = G(lambda e, hh=hh: e.tensor_copy(
                t["wsm_b"][:, hh * 1024:(hh + 1) * 1024], t["wst_f"][hh][:]))
        self.fire(ins, "po")
        self.w("vector", "po")
        V(lambda e: e.memset(t["nega"][:], 0.0))
        V(lambda e: e.memset(t["S_HB"][:], 0.0))
        V(lambda e: e.memset(t["tr_row"][:], 0.0))
        self.fire(V(lambda e: e.memset(t["ztails"][:], 0.0)), "dv")
        self.w("scalar", "dv")
        self.fire(self.A(lambda e: e.activation(out=t["nega"][32:48, 0:1], in_=t["sv_col"][32:48, 1:2],
                                                func=AF.Exp)), "ac")
        self.w("vector", "ac")
        self.fire(V(lambda e: e.tensor_scalar_mul(t["nega"][32:48, 0:1], t["nega"][32:48, 0:1], -1.0)), "dv")
        self.w("gpsimd", "dv")
        self.fire(self.G(lambda e: e.dma_start(out=t["nega"][0:16, 1:2], in_=t["nega"][32:48, 0:1])), "sp", 16)
        for who in ("tensor", "scalar", "vector", "sync"):
            self.w(who, "sp")
        for who in ("tensor", "scalar", "gpsimd", "sync"):
            self.w(who, "dv")

    def mask_states(self):
        t = self.t
        self.w("vector", "ac")
        self.w("vector", "pe")
        for hblk in range(2):
            sl = t["S_HB"][:, hblk * 2048:(hblk + 1) * 2048]
            self.fire(self.V(lambda e, s=sl: e.tensor_scalar_mul(s, s, t["pm_col"][:, 0:1])), "dv")
        self.fire(self.V(lambda e: e.tensor_single_scalar(
            t["tr_row"][:], t["tr_row"][:], t["pm_col"][0:1, 0:1], ALU.mult)), "dv")
        for who in ("tensor", "scalar", "gpsimd"):
            self.w(who, "dv")

    def ln_quarter(self, xsrc, q, halo):
        t = self.t
        PX, PC = t["PX"], t["PC"]
        self.w("tensor", "dv")
        self.w("tensor", "ac")
        for dt in range(KT):
            par = dt % 2
            if dt >= 2:
                self.w("sync", "ac")
                self.w("sync", "pe")
            self.fire(self.S(lambda e, dt=dt, par=par: e.dma_start(
                out=t["xq"][par][:], in_=xsrc[dt * 128:(dt + 1) * 128, 3 + q * QT:3 + (q + 1) * QT])), "sx", 16)
            self.w("scalar", "sx")
            self.fire(self.A(lambda e, par=par: e.activation(out=t["sqt"][:], in_=t["xq"][par][:],
                                                             func=AF.Square)), "ac")
            self.w("tensor", "ac")
            if LNMM:
                self.P(lambda e, dt=dt, par=par: e.matmul(PX[0:1, 0:QT], t["ones_c"][:], t["xq"][par][:],
                                                          start=(dt == 0), stop=(dt == KT - 1)))
                self.fire(self.P(lambda e, dt=dt: e.matmul(PC[0:1, 0:QT], t["ones_c"][:], t["sqt"][:],
                                                           start=(dt == 0), stop=(dt == KT - 1))), "pe")
            else:
                self.fire(self.P(lambda e: e.nop()), "pe")
        if LNPARTS < 2:
            return
        self.w("scalar", "pe")
        self.fire(self.A(lambda e: e.activation(out=t["m_row"][:], in_=PX[0:1, 0:QT],
                                                func=AF.Copy, scale=t["kcc"][0:1, 3:4])), "ac")
        self.fire(self.A(lambda e: e.activation(out=t["sq_row"][:], in_=PC[0:1, 0:QT],
                                                func=AF.Copy, scale=t["kcc"][0:1, 3:4])), "ac")
        self.fire(self.A(lambda e: e.activation(out=t["rs_row"][:], in_=t["m_row"][:],
                                                func=AF.Square)), "ac")
        self.w("vector", "ac")
        self.fire(self.V(lambda e: e.tensor_sub(t["sq_row"][:], t["sq_row"][:], t["rs_row"][:])), "dv")
        self.w("scalar", "dv")
        self.fire(self.A(lambda e: e.activation(out=t["rs_row"][:], in_=t["sq_row"][:],
                                                func=AF.Sqrt, bias=t["kcc"][0:1, 0:1])), "ac")
        self.w("vector", "ac")
        self.fire(self.V(lambda e: e.reciprocal(t["rs_row"][:], t["rs_row"][:])), "dv")
        if LNPARTS < 3:
            return
        self.w("tensor", "dv")
        self.P(lambda e: e.matmul(t["PB"][:, 0:QT], t["ones_f"][:], t["m_row"][:], start=True, stop=True))
        self.fire(self.P(lambda e: e.matmul(t["PB"][:, 512:512 + QT], t["ones_f"][:], t["rs_row"][:],
                                            start=True, stop=True)), "pe")
        self.w("scalar", "pe")
        self.A(lambda e: e.activation(out=t["m_bc"][:], in_=t["PB"][:, 0:QT], func=AF.Copy))
        self.fire(self.A(lambda e: e.activation(out=t["rs_bc"][:], in_=t["PB"][:, 512:512 + QT],
                                                func=AF.Copy)), "ac")
        if q == 0 and halo and LNPARTS >= 4:
            self.w("sync", "ac", self.c["ac"].v)
            for dt in range(KT):
                self.fire(self.S(lambda e, dt=dt: e.dma_start(
                    out=t["xh_t"][:, dt * 3:(dt + 1) * 3], in_=xsrc[dt * 128:(dt + 1) * 128, 0:3])), "sx", 16)
            self.w("scalar", "sx")
            self.fire(self.A(lambda e: e.activation(out=t["sqh"][:], in_=t["xh_t"][:], func=AF.Square)), "ac")
            self.w("tensor", "ac")
            for dt in range(KT):
                self.P(lambda e, dt=dt: e.matmul(PX[0:1, 256:259], t["ones_c"][:],
                                                 t["xh_t"][:, dt * 3:(dt + 1) * 3],
                                                 start=(dt == 0), stop=(dt == KT - 1)))
                self.fire(self.P(lambda e, dt=dt: e.matmul(PC[0:1, 256:259], t["ones_c"][:],
                                                           t["sqh"][:, dt * 3:(dt + 1) * 3],
                                                           start=(dt == 0), stop=(dt == KT - 1))), "pe")
            self.w("scalar", "pe")
            self.A(lambda e: e.activation(out=t["mh_row"][:, 0:3], in_=PX[0:1, 256:259],
                                          func=AF.Copy, scale=t["kcc"][0:1, 3:4]))
            self.fire(self.A(lambda e: e.activation(out=t["vh_row"][:, 0:3], in_=PC[0:1, 256:259],
                                                    func=AF.Copy, scale=t["kcc"][0:1, 3:4])), "ac")
            self.w("vector", "ac")
            self.V(lambda e: e.tensor_tensor(t["rsh_row"][:, 0:3], t["mh_row"][:, 0:3],
                                             t["mh_row"][:, 0:3], ALU.mult))
            self.V(lambda e: e.tensor_sub(t["vh_row"][:, 0:3], t["vh_row"][:, 0:3], t["rsh_row"][:, 0:3]))
            self.fire(self.V(lambda e: e.tensor_scalar_add(t["vh_row"][:, 0:3], t["vh_row"][:, 0:3],
                                                           LN_EPS)), "dv")
            self.w("scalar", "dv")
            self.fire(self.A(lambda e: e.activation(out=t["rsh_row"][:, 0:3], in_=t["vh_row"][:, 0:3],
                                                    func=AF.Sqrt)), "ac")
            self.w("vector", "ac")
            self.fire(self.V(lambda e: e.reciprocal(t["rsh_row"][:, 0:3], t["rsh_row"][:, 0:3])), "dv")
            self.w("tensor", "dv")
            self.P(lambda e: e.matmul(PX[:, 256:259], t["ones_f"][:], t["mh_row"][:, 0:3],
                                      start=True, stop=True))
            self.fire(self.P(lambda e: e.matmul(PC[:, 256:259], t["ones_f"][:], t["rsh_row"][:, 0:3],
                                                start=True, stop=True)), "pe")
            self.w("vector", "pe")
            self.V(lambda e: e.tensor_sub(
                t["xh_f"][:].rearrange("p (a b) -> p a b", b=3), t["xh_t"][:].rearrange("p (a b) -> p a b", b=3),
                PX[:, 256:259].unsqueeze(1).broadcast_to([128, KT, 3])))
            self.fire(self.V(lambda e: e.tensor_mul(
                t["xnh"][:].rearrange("p (a b) -> p a b", b=3), t["xh_f"][:].rearrange("p (a b) -> p a b", b=3),
                PC[:, 256:259].unsqueeze(1).broadcast_to([128, KT, 3]))), "dv")
            for who in ("tensor", "scalar", "gpsimd", "sync"):
                self.w(who, "dv")
        if LNPARTS < 5:
            return
        for dt in range(KT):
            par = dt % 2
            if dt >= 2:
                self.w("sync", "po")
            self.fire(self.S(lambda e, dt=dt, par=par: e.dma_start(
                out=t["xq"][par][:], in_=xsrc[dt * 128:(dt + 1) * 128, 3 + q * QT:3 + (q + 1) * QT])), "sx", 16)
            self.w("gpsimd", "sx")
            self.w("gpsimd", "ac")
            self.G(lambda e, par=par: e.tensor_sub(t["xfw"][:], t["xq"][par][:], t["m_bc"][:]))
            self.fire(self.G(lambda e, dt=dt: e.tensor_mul(
                t["xn1"][:, dt * QT:(dt + 1) * QT], t["xfw"][:], t["rs_bc"][:])), "po")
        for who in ("tensor", "sync"):
            self.w(who, "po")

    def stream_col(self, dram_ap, col_idx, n_prev_cols):
        t = self.t
        par = n_prev_cols % 2
        hw = dram_ap.shape[-1] // 2
        for hh in range(2):
            if n_prev_cols >= 2 or hh == 1:
                self.w("sync", "po")
            self.fire(self.S(lambda e, hh=hh: e.dma_start(
                out=t["wst_f"][par][:, 0:hw], in_=dram_ap[:, hh * hw:(hh + 1) * hw])), "sx", 16)
            if n_prev_cols >= 2 and hh == 0:
                self.w("gpsimd", "pe", self.c["pe"].v)
            self.w("gpsimd", "sx")
            self.fire(self.G(lambda e, hh=hh: e.tensor_copy(
                t["wst_b"][par][:, hh * hw:(hh + 1) * hw], t["wst_f"][par][:, 0:hw])), "po")
        self.w("tensor", "po")
        return par

    def stream_load_small(self, dram_ap, dest_bf, scale, rows, ctr):
        t = self.t
        par = ctr[0] % 2
        W = dram_ap.shape[-1]
        hw = W // 2
        for hh in range(2):
            if ctr[0] >= 2 or hh == 1:
                self.w("sync", "po")
            self.fire(self.S(lambda e, hh=hh: e.dma_start(
                out=t["wst_f"][par][0:rows, 0:hw], in_=dram_ap[:, hh * hw:(hh + 1) * hw])), "sx", 16)
            if ctr[0] >= 2 and hh == 0:
                self.w("gpsimd", "pe", self.c["pe"].v)
            self.w("gpsimd", "sx")
            if scale == 1.0:
                self.fire(self.G(lambda e, hh=hh: e.tensor_copy(
                    dest_bf[:, hh * hw:(hh + 1) * hw], t["wst_f"][par][0:rows, 0:hw])), "po")
            else:
                self.fire(self.G(lambda e, hh=hh: e.tensor_scalar_mul(
                    dest_bf[:, hh * hw:(hh + 1) * hw], t["wst_f"][par][0:rows, 0:hw], scale)), "po")
        ctr[0] += 1
        return par

    def conv_silu(self, q, dest_tile, conv_c, ot, tail_idx, halo_src, out_scale_none=True):
        t = self.t
        tails = t["ztails"][:, tail_idx * 3:(tail_idx + 1) * 3]
        if q == 0:
            pass
        else:
            self.fire(self.V(lambda e: e.tensor_copy(t["zbuf"][:, 0:3], tails)), "dv")
        self.fire(self.V(lambda e: e.tensor_copy(tails, t["zbuf"][:, QT:QT + 3])), "dv")
        wc = lambda j: conv_c[:, ot * CONV + j:ot * CONV + j + 1]
        self.V(lambda e: e.tensor_scalar(t["zacc"][:], t["zbuf"][:, 0:QT], wc(0), None, ALU.mult))
        lastv = None
        for j in range(1, CONV):
            lastv = self.V(lambda e, j=j: e.scalar_tensor_tensor(
                out=t["zacc"][:], in0=t["zbuf"][:, j:j + QT], scalar=wc(j), in1=t["zacc"][:],
                op0=ALU.mult, op1=ALU.add)) or lastv
        self.fire(lastv, "dv")
        self.w("scalar", "dv")
        self.fire(self.A(lambda e: e.activation(out=dest_tile, in_=t["zacc"][:], func=AF.Silu)), "ac")

    def proj_col(self, w_dram, ot, q, dest_tile, conv_c, tail_idx, ncol_ctr):
        t = self.t
        par = self.stream_col(w_dram[ot], ot, ncol_ctr[0])
        ncol_ctr[0] += 1
        PC = t["PC"]
        self.w("tensor", "ac")
        self.w("tensor", "dv")
        lastmm = None
        for kt in range(KT):
            lastmm = self.P(lambda e, kt=kt, par=par: e.matmul(
                PC[:, 0:QT], t["wst_b"][par][:, kt * 128:(kt + 1) * 128],
                t["xn1"][:, kt * QT:(kt + 1) * QT], start=(kt == 0), stop=(kt == KT - 1))) or lastmm
        if q == 0 and PCPARTS != 18:
            for kt in range(KT):
                lastmm = self.P(lambda e, kt=kt, par=par: e.matmul(
                    t["PB"][:, 0:3], t["wst_b"][par][:, kt * 128:(kt + 1) * 128],
                    t["xnh"][:, kt * 3:(kt + 1) * 3], start=(kt == 0), stop=(kt == KT - 1))) or lastmm
        self.fire(lastmm, "pe")
        if PCPARTS < 2:
            return
        self.w("scalar", "pe")
        self.w("scalar", "dv")
        if PCPARTS == 15:
            self.fire(self.A(lambda e: e.activation(out=t["sqt"][:], in_=PC[:, 0:QT],
                                                    func=AF.Copy)), "ac")
            return
        if PCPARTS in (17, 18):
            self.w("vector", "pe")
            self.fire(self.V(lambda e: e.tensor_copy(t["zacc"][:], PC[:, 0:QT])), "dv")
            return
        self.fire(self.A(lambda e: e.activation(out=t["zbuf"][:, 3:3 + QT], in_=PC[:, 0:QT],
                                                func=AF.Copy)), "ac")
        if q == 0 and PCPARTS >= 21:
            self.fire(self.A(lambda e: e.activation(out=t["zbuf"][:, 0:3], in_=t["PB"][:, 0:3],
                                                    func=AF.Copy)), "ac")
        if PCPARTS < 3:
            self.w("vector", "ac")
            self.fire(self.V(lambda e: e.tensor_copy(dest_tile, t["zbuf"][:, 3:3 + QT])), "dv")
            return
        self.w("vector", "ac")
        self.conv_silu(q, dest_tile, conv_c, ot, tail_idx, None)

    def quarter(self, xsrc, q, main):
        t = self.t
        self.ln_quarter(xsrc, q, halo=True)
        ncol = [0]
        if QPARTS < 2:
            return
        for ot in range(KV_OT):
            self.proj_col(t["wk_t"], ot, q, t["kfm"][:, ot * QT:(ot + 1) * QT], t["cvk_c"], KT + ot, ncol)
            if QPARTS >= 3:
                self.transpose_kv(ot, q, t["kfm"][:, ot * QT:(ot + 1) * QT], "ktok")
        if QPARTS < 4:
            return
        for ot in range(KV_OT):
            self.proj_col(t["wv_t"], ot, q, t["vfm_s"][:], t["cvv_c"], KT + KV_OT + ot, ncol)
            self.transpose_kv(ot, q, t["vfm_s"][:], "vtok")
        if QPARTS < 5:
            return
        self.small_col(q, ncol)
        if QPARTS < 6:
            return
        self.stats_quarter(q, main)
        if QPARTS < 7:
            return
        if main:
            for ot in range(KT):
                self.proj_col(t["wq_t"], ot, q, t["qhat"][:, ot * QT:(ot + 1) * QT], t["cvq_c"], ot, ncol)
            for ot in range(KT):
                par = self.stream_col(t["wg_t"][ot], ot, ncol[0])
                ncol[0] += 1
                self.w("tensor", "ac")
                lastmm = None
                for kt in range(KT):
                    lastmm = self.P(lambda e, kt=kt, par=par: e.matmul(
                        t["PC"][:, 0:QT], t["wst_b"][par][:, kt * 128:(kt + 1) * 128],
                        t["xn1"][:, kt * QT:(kt + 1) * QT], start=(kt == 0), stop=(kt == KT - 1))) or lastmm
                self.fire(lastmm, "pe")
                self.w("scalar", "pe")
                self.fire(self.A(lambda e, ot=ot: e.activation(
                    out=t["gsil"][:, ot * QT:(ot + 1) * QT], in_=t["PC"][:, 0:QT], func=AF.Silu)), "ac")
                self.w("tensor", "ac")
            self.coef_quarter(q)
            for n in range(NCH_Q):
                self.chunk(q, n)
            if q == 0:
                self.tap("ofin", lambda: t["ofin"][:])
                self.tap("gsil", lambda: t["gsil"][:])
                self.tap("qhat", lambda: t["qhat"][:])
                self.tap("Z_all", lambda: t["Z_all"][:])
                self.tap("o1", lambda: t["o1"][:])
                self.tap("opair", lambda: t["opair"][:])
                self.tap("qt_b", lambda: t["qt_b"][:])
                self.tap("A_b", lambda: t["A_b"][:])
                self.tap("nB0", lambda: t["nB0"][:])
                self.tap("scm", lambda: t["scm"][:])
                self.tap("statsT", lambda: t["statsT"][:])
                self.tap("ktok", lambda: t["ktok"][:])
                self.tap("S_HB", lambda: t["S_HB"][:])
                self.tap("dmat", lambda: t["dmat"][:])
                self.tap("lmax", lambda: t["lmax_row"][:])
                self.tap("coef1", lambda: t["coef"][1][:])
                self.tap("dgq", lambda: t["dgq_row"][:])
                self.tap("trow", lambda: t["tr_row"][:])
            self.wo_quarter(q)
        else:
            for n in range(NCH_Q):
                self.chunk_scan_only(q, n)

    def transpose_kv(self, ot, q, src_tile, dest_name):
        t = self.t
        dest = t[dest_name]
        ptb = t["PX"][:, 320:384].bitcast(BF16)
        for g in range(QT // 128):
            self.w("tensor", "ac")
            self.w("tensor", "dv")
            self.P(lambda e, g=g: e.transpose(
                ptb, src_tile[:, g * 128:(g + 1) * 128], t["ident_b"][:]))
            self.fire(self.P(lambda e: e.matmul(
                t["PX"][0:1, 448:449], t["ones_f"][:, 0:1], t["ones_f"][:, 0:1],
                start=True, stop=True)), "pe")
            self.w("vector", "pe")
            n0 = 2 * g
            self.V(lambda e, n0=n0: e.tensor_copy(
                dest[:, (ot * NCH_Q + n0) * 128:(ot * NCH_Q + n0 + 1) * 128], ptb[0:64, :]))
            self.fire(self.V(lambda e, n0=n0: e.tensor_copy(
                dest[:, (ot * NCH_Q + n0 + 1) * 128:(ot * NCH_Q + n0 + 2) * 128], ptb[64:128, :])), "dv")

    def small_col(self, q, ncol):
        t = self.t
        self.w("tensor", "dv")
        self.w("tensor", "ac")
        lastmm = None
        for kt in range(KT):
            lastmm = self.P(lambda e, kt=kt: e.matmul(
                t["PC"][:, 0:QT], t["wsm_b"][:, kt * 128:(kt + 1) * 128],
                t["xn1"][:, kt * QT:(kt + 1) * QT], start=(kt == 0), stop=(kt == KT - 1))) or lastmm
        self.fire(lastmm, "pe")
        self.w("scalar", "pe")
        sv = t["sv_col"]
        self.A(lambda e: e.activation(out=t["beta_t"][:], in_=t["PC"][0:16, 0:QT],
                                      func=AF.Sigmoid, bias=sv[0:16, 0:1]))
        self.fire(self.A(lambda e: e.activation(out=t["alpha_t"][:], in_=t["PC"][64:80, 0:QT],
                                                func=AF.Sigmoid, bias=sv[64:80, 0:1])), "ac")
        self.w("vector", "ac", self.c["ac"].v)
        self.fire(self.V(lambda e: e.tensor_scalar(t["cs_t"][:], t["PC"][32:48, 0:QT],
                                                   sv[32:48, 0:1], None, ALU.add)), "dv")
        self.w("scalar", "dv")
        self.A(lambda e: e.activation(out=t["wend_t"][:], in_=t["cs_t"][:], func=AF.Abs))
        self.fire(self.A(lambda e: e.activation(out=t["wend_t"][:], in_=t["wend_t"][:],
                                                func=AF.Exp, scale=t["kcc"][0:16, 4:5])), "ac")
        self.w("scalar", "ac", self.c["ac"].v)
        self.fire(self.A(lambda e: e.activation(out=t["wend_t"][:], in_=t["wend_t"][:],
                                                func=AF.Ln, bias=t["kcc"][0:16, 1:2])), "ac")
        self.w("vector", "ac")
        self.V(lambda e: e.tensor_scalar_max(t["glog_t"][:], t["cs_t"][:], 0.0))
        self.V(lambda e: e.tensor_add(t["glog_t"][:], t["glog_t"][:], t["wend_t"][:]))
        self.fire(self.V(lambda e: e.tensor_scalar(t["glog_t"][:], t["glog_t"][:],
                                                   t["nega"][0:16, 1:2], None, ALU.mult)), "dv")
        self.w("tensor", "dv")

    def stats_quarter(self, q, main):
        t = self.t
        cur, oth = t["spx"][0:16, :], t["xfw"][0:16, :]
        self.fire(self.V(lambda e: e.tensor_copy(cur[:], t["glog_t"][:])), "dv")
        sh = 1
        while sh < C:
            s3 = lambda x, a, b: x.rearrange("p (n c) -> p n c", c=C)[:, :, a:b]
            self.V(lambda e, c=cur, o=oth, sh=sh: e.tensor_copy(s3(o, 0, sh), s3(c, 0, sh)))
            self.fire(self.V(lambda e, c=cur, o=oth, sh=sh: e.tensor_add(
                s3(o, sh, C), s3(c, sh, C), s3(c, 0, C - sh))), "dv")
            cur, oth = oth, cur
            sh *= 2
        self.fire(self.V(lambda e, c=cur: e.tensor_copy(t["cs_t"][:], c[:])), "dv")
        gview = t["cs_t"][:].rearrange("p (n c) -> p n c", c=C)[:, :, C - 1:C]
        self.fire(self.V(lambda e: e.tensor_sub(
            t["wend_t"][:].rearrange("p (n c) -> p n c", c=C),
            gview.broadcast_to([16, NCH_Q, C]),
            t["cs_t"][:].rearrange("p (n c) -> p n c", c=C))), "dv")
        self.w("scalar", "dv")
        self.fire(self.A(lambda e: e.activation(out=t["wend_t"][:], in_=t["wend_t"][:], func=AF.Exp)), "ac")
        self.w("vector", "ac")
        self.fire(self.V(lambda e: e.tensor_mul(t["wend_t"][:], t["wend_t"][:], t["beta_t"][:])), "dv")
        self.V(lambda e: e.tensor_copy(t["comb_t"][0:16, 0:QT], t["cs_t"][:]))
        self.V(lambda e: e.tensor_scalar_mul(t["comb_t"][32:48, 0:QT], t["beta_t"][:], SQ))
        self.V(lambda e: e.tensor_copy(t["comb_t"][64:80, 0:QT], t["wend_t"][:]))
        self.fire(self.V(lambda e: e.tensor_copy(t["comb_t"][96:112, 0:QT], t["alpha_t"][:])), "dv")
        ptf = t["PX"][:, 384:416]
        ptf = t["PX"][:, 384:512]
        for g in range(QT // 128):
            self.w("tensor", "dv")
            self.w("tensor", "ac")
            self.P(lambda e, g=g: e.transpose(
                ptf, t["comb_t"][:, g * 128:(g + 1) * 128], t["ident_f"][:]))
            self.fire(self.P(lambda e: e.matmul(
                t["PX"][0:1, 448:449], t["ones_f"][:, 0:1], t["ones_f"][:, 0:1],
                start=True, stop=True)), "pe")
            self.w("vector", "pe")
            self.V(lambda e, g=g: e.tensor_copy(t["statsT"][:, (2 * g) * 128:(2 * g + 1) * 128],
                                                ptf[0:64, :]))
            self.fire(self.V(lambda e, g=g: e.tensor_copy(t["statsT"][:, (2 * g + 1) * 128:(2 * g + 2) * 128],
                                                          ptf[64:128, :])), "dv")
        self.w("gpsimd", "dv")
        for h in range(16):
            self.fire(self.G(lambda e, h=h: e.dma_start(
                out=t["csr"][0:1, h * QT:(h + 1) * QT], in_=t["cs_t"][h:h + 1, :])), "sp", 16)
        for who in ("vector", "tensor", "scalar"):
            self.w(who, "sp")
        self.fire(self.A(lambda e: e.activation(
            out=t["dgq_row"][:].rearrange("a (h n) -> a h n", n=NCH_Q),
            in_=t["csr"][:].rearrange("a (h n c) -> a h n c", n=NCH_Q, c=C)[:, :, :, 63],
            func=AF.Exp)), "ac")
        for who in ("vector", "tensor", "gpsimd"):
            self.w(who, "ac")
        for n in range(NCH_Q):
            for half in range(2):
                for kv in range(half * 4, half * 4 + 4):
                    self.fire(self.A(lambda e, kv=kv, n=n, half=half: e.activation(
                        out=t["osq"][:, (kv - half * 4) * 128:(kv - half * 4 + 1) * 128],
                        in_=t["ktok"][:, (kv * NCH_Q + n) * 128:(kv * NCH_Q + n + 1) * 128],
                        func=AF.Square)), "ac")
                self.w("vector", "ac")
                self.fire(self.V(lambda e, half=half: e.tensor_reduce(
                    t["ksq_t"][:, half * 4:(half + 1) * 4].unsqueeze(2),
                    t["osq"][:, 0:512].rearrange("p (a b) -> p a b", b=128),
                    mybir.AxisListType.X, ALU.add)), "dv")
                self.w("scalar", "dv")
            self.w("gpsimd", "dv")
            self.G(lambda e, n=n: e.tensor_mul(
                t["trv_t"][:].rearrange("p (a b) -> p a b", b=2)[:, :, 0:1],
                t["statsT"][:, n * 128 + 64:n * 128 + 80].rearrange("p (a b) -> p a b", b=2)[:, :, 0:1],
                t["ksq_t"][:].unsqueeze(2)))
            self.fire(self.G(lambda e, n=n: e.tensor_mul(
                t["trv_t"][:].rearrange("p (a b) -> p a b", b=2)[:, :, 1:2],
                t["statsT"][:, n * 128 + 64:n * 128 + 80].rearrange("p (a b) -> p a b", b=2)[:, :, 1:2],
                t["ksq_t"][:].unsqueeze(2))), "po")
            self.w("tensor", "po")
            self.fire(self.P(lambda e: e.matmul(t["PX"][0:1, 280:296], t["ones64"][:], t["trv_t"][:],
                                                start=True, stop=True)), "pe")
            self.fire(self.P(lambda e: e.matmul(t["PX"][0:1, 448:464], t["ones64"][:],
                                                t["trv_t"][:], start=True, stop=True)), "pe")
            self.w("scalar", "pe")
            self.fire(self.A(lambda e: e.activation(out=t["trdh_row"][:], in_=t["PX"][0:1, 280:296],
                                                    func=AF.Copy)), "ac")
            self.w("vector", "ac")
            if main:
                self.fire(self.V(lambda e, n=n: e.tensor_scalar_add(
                    t["lmax_row"][:, n * 16:(n + 1) * 16], t["tr_row"][:], RIDGE * HD)), "dv")
            self.fire(self.G(lambda e, n=n: e.tensor_mul(t["tr_tmp"][:], t["tr_row"][:],
                                                         t["dgq_row"][0:1, n:n + 15 * NCH_Q + 1:NCH_Q])), "po")
            self.w("vector", "po")
            self.fire(self.V(lambda e: e.tensor_add(t["tr_row"][:], t["tr_tmp"][:], t["trdh_row"][:])), "dv")
            if main and q == 0 and n == 0:
                self.tap("ksqA", lambda: t["ksq_t"][:])
                self.tap("trvA", lambda: t["trv_t"][:])
                self.tap("tdA", lambda: t["trdh_row"][:])

    def coef_quarter(self, q):
        t = self.t
        V, A, G = self.V, self.A, self.G
        u, bf = t["rho_r"][0], t["rho_r"][1]
        Tc = lambda i: t["b_tab"][:, (i - 2) * W_CH:(i - 1) * W_CH]
        V(lambda e: e.tensor_scalar(t["th_row"][:], t["lmax_row"][:], RIDGE, -0.5, ALU.add, ALU.mult))
        self.fire(V(lambda e: e.tensor_scalar(t["dl_row"][:], t["lmax_row"][:], -RIDGE, -0.5,
                                              ALU.add, ALU.mult)), "dv")
        V(lambda e: e.tensor_copy(u[:], t["lmax_row"][:]))
        V(lambda e: e.tensor_copy(bf[:], t["lmax_row"][:]))
        V(lambda e: e.reciprocal(t["th_tab"][:], t["th_row"][:]))
        self.fire(V(lambda e: e.reciprocal(t["dl_row"][:], t["dl_row"][:])), "dv")
        self.w("gpsimd", "dv")
        G(lambda e: e.tensor_mul(t["s1_row"][:], t["th_row"][:], t["dl_row"][:]))
        self.fire(G(lambda e: e.tensor_scalar_mul(bf[:], t["dl_row"][:], 2.0)), "po")
        self.w("vector", "po")
        V(lambda e: e.tensor_mul(t["th_row"][:], t["s1_row"][:], t["s1_row"][:]))
        V(lambda e: e.tensor_scalar(Tc(2), t["th_row"][:], 2.0, -1.0, ALU.mult, ALU.add))
        V(lambda e: e.tensor_scalar_mul(u[:], t["s1_row"][:], 2.0))
        for i in range(3, NITER + 2):
            V(lambda e, i=i: e.tensor_mul(t["th_row"][:], u[:], Tc(i - 1)))
            V(lambda e, i=i: e.tensor_sub(
                Tc(i), t["th_row"][:], t["s1_row"][:] if i == 3 else Tc(i - 2)))
        V(lambda e: e.tensor_copy(u[:], t["s1_row"][:]))
        V(lambda e: e.tensor_scalar_mul(u[:], u[:], 2.0))
        HW2 = (NITER // 2) * W_CH
        V(lambda e: e.reciprocal(t["a_tab"][:, HW2:NITER * W_CH], t["b_tab"][:, HW2:NITER * W_CH]))
        self.fire(V(lambda e: e.reciprocal(t["a_tab"][:, 0:HW2], t["b_tab"][:, 0:HW2])), "dv")
        for i in range(NITER - 1, -1, -1):
            V(lambda e, i=i: e.tensor_mul(t["th_row"][:], bf[:],
                                          t["s1_row"][:] if i == 0 else Tc(i + 1)))
            ins = V(lambda e, i=i: e.tensor_mul(
                t["b_tab"][:, i * W_CH:(i + 1) * W_CH], t["th_row"][:],
                t["a_tab"][:, i * W_CH:(i + 1) * W_CH]))
            if i == 1:
                ins = V(lambda e: e.tensor_mul(t["a_tab"][:, W_CH:2 * W_CH], t["s1_row"][:],
                                               t["a_tab"][:, W_CH:2 * W_CH]))
            elif i >= 2:
                ins = V(lambda e, i=i: e.tensor_mul(
                    t["a_tab"][:, i * W_CH:(i + 1) * W_CH], Tc(i),
                    t["a_tab"][:, i * W_CH:(i + 1) * W_CH]))
        self.fire(ins, "dv")
        for who in ("tensor", "scalar"):
            self.w(who, "dv")

    def chunk_prep(self, q, n, main):
        t = self.t
        cpar = n % 2
        cf = t["coef"][cpar]
        for who in ("tensor", "scalar", "vector", "gpsimd"):
            for cn in ("pe", "ac", "dv", "po", "cp0", "cp1", "ca0", "ca1", "cv0", "cv1"):
                self.w(who, cn)
        for hb in range(2):
            shb = t["S_HB"][:].rearrange("p (h x) -> p h x", x=256)
            self.V(lambda e, hb=hb: e.scalar_tensor_tensor(
                out=t["A_b"][:].rearrange("p (h x) -> p h x", x=128)[:, hb * 8:(hb + 1) * 8, :],
                in0=t["ident_f"][:].unsqueeze(1).broadcast_to([128, 8, 128]),
                scalar=RIDGE, in1=shb[:, hb * 8:(hb + 1) * 8, 0:128],
                op0=ALU.mult, op1=ALU.add))
            self.fire(self.V(lambda e, hb=hb: e.tensor_scalar_mul(
                t["nB0"][:].rearrange("p (h x) -> p h x", x=128)[:, hb * 8:(hb + 1) * 8, :],
                shb[:, hb * 8:(hb + 1) * 8, 128:256], -SQ)), "dv")
        if main:
            self.w("tensor", "dv")
            a3 = t["a_tab"][:].rearrange("a (i h) -> a i h", h=W_CH)[:, :, n * 16:(n + 1) * 16]
            b3 = t["b_tab"][:].rearrange("a (i h) -> a i h", h=W_CH)[:, :, n * 16:(n + 1) * 16]
            self.P(lambda e: e.matmul(t["PB"][:, 0:464], t["ones_f"][:], a3, start=True, stop=True))
            self.P(lambda e: e.matmul(t["PB"][:, 512:976], t["ones_f"][:], b3, start=True, stop=True))
            self.P(lambda e: e.matmul(t["PB"][:, 976:992], t["ones_f"][:],
                                      t["th_tab"][:, n * 16:(n + 1) * 16], start=True, stop=True))
            self.fire(self.P(lambda e: e.matmul(t["PB"][:, 992:1008], t["ones_f"][:],
                                                t["dgq_row"][0:1, n:n + 15 * NCH_Q + 1:NCH_Q],
                                                start=True, stop=True)), "pe")
            self.w("scalar", "pe")
            self.A(lambda e: e.activation(out=cf[:, COEF_A:COEF_A + 464], in_=t["PB"][:, 0:464],
                                          func=AF.Copy))
            self.A(lambda e: e.activation(out=cf[:, COEF_B:COEF_B + 464], in_=t["PB"][:, 512:976],
                                          func=AF.Copy))
            self.A(lambda e: e.activation(out=cf[:, COEF_TH:COEF_TH + 16], in_=t["PB"][:, 976:992],
                                          func=AF.Copy))
            self.fire(self.A(lambda e: e.activation(out=cf[:, COEF_DG:COEF_DG + 16],
                                                    in_=t["PB"][:, 992:1008], func=AF.Copy)), "ac")
        else:
            self.w("tensor", "dv")
            self.fire(self.P(lambda e: e.matmul(t["PB"][:, 992:1008], t["ones_f"][:],
                                                t["dgq_row"][0:1, n:n + 15 * NCH_Q + 1:NCH_Q],
                                                start=True, stop=True)), "pe")
            self.w("scalar", "pe")
            self.fire(self.A(lambda e: e.activation(out=cf[:, COEF_DG:COEF_DG + 16],
                                                    in_=t["PB"][:, 992:1008], func=AF.Copy)), "ac")
        if main:
            self.w("tensor", "ac")
            cs3 = t["csr"][:].rearrange("a (h t) -> a h t", t=QT)[:, :, n * 64:(n + 1) * 64]
            self.P(lambda e: e.matmul(t["PB"][:, 0:512], t["ones_f"][:],
                                      cs3[:, 0:8, :], start=True, stop=True))
            self.fire(self.P(lambda e: e.matmul(t["PB"][:, 512:1024], t["ones_f"][:],
                                                cs3[:, 8:16, :], start=True, stop=True)), "pe")
            self.w("vector", "pe")
            for h in range(16):
                self.fire(self.V(lambda e, h=h: e.tensor_scalar(
                    t["dmat"][:, h * 64:(h + 1) * 64], t["PB"][0:64, h * 64:(h + 1) * 64],
                    t["statsT"][:, n * 128 + h:n * 128 + h + 1], None, ALU.subtract)), "dv")
            self.w("scalar", "dv")
            self.A(lambda e: e.activation(out=t["PB"][:, 0:512], in_=t["PB"][:, 0:512], func=AF.Exp))
            self.fire(self.A(lambda e: e.activation(out=t["PB"][:, 512:1024], in_=t["PB"][:, 512:1024],
                                                    func=AF.Exp)), "ac")
            self.w("vector", "ac")
            q3 = t["qhat"][:].rearrange("p (h tt) -> p h tt", tt=QT)[:, :, n * 64:(n + 1) * 64]
            self.fire(self.V(lambda e: e.tensor_mul(
                t["qt_b"][:].rearrange("p (h x) -> p h x", x=64), q3,
                t["PB"][:].rearrange("p (h x) -> p h x", x=64))), "dv")
            self.fire(self.V(lambda e: e.tensor_tensor(
                t["dmat"][:].rearrange("p (h x) -> p h x", x=64),
                t["dmat"][:].rearrange("p (h x) -> p h x", x=64),
                t["mask64"][:].unsqueeze(1).broadcast_to([64, 16, 64]), ALU.min)), "dv")
            self.w("scalar", "dv")
            self.fire(self.A(lambda e: e.activation(out=t["dmat"][:], in_=t["dmat"][:], func=AF.Exp)), "ac")
            self.w("vector", "ac")
            for h in range(16):
                self.fire(self.V(lambda e, h=h: e.tensor_scalar(
                    t["dmat"][:, h * 64:(h + 1) * 64], t["dmat"][:, h * 64:(h + 1) * 64],
                    t["statsT"][:, n * 128 + 32 + h:n * 128 + 33 + h], None, ALU.mult)), "dv")
        self.w("tensor", "dv")
        self.w("tensor", "ac")
        base_po = self.c["po"].v
        base_pe = self.c["pe"].v
        base_dv = self.c["dv"].v
        for h in range(16):
            kv = h // 2
            ksl = t["ktok"][:, (kv * NCH_Q + n) * 128:(kv * NCH_Q + n + 1) * 128]
            vsl = t["vtok"][:, (kv * NCH_Q + n) * 128:(kv * NCH_Q + n + 1) * 128]
            wpar = h % 2
            if h >= 2:
                self.w("gpsimd", "pe", base_pe + h - 1)
            self.fire(self.G(lambda e, h=h, wpar=wpar, ksl=ksl: e.tensor_scalar(
                t["wk_s"][:, wpar * 128:(wpar + 1) * 128], ksl,
                t["statsT"][:, n * 128 + 64 + h:n * 128 + 65 + h], None, ALU.mult)), "po")
            self.w("tensor", "po", base_po + h + 1)
            if h >= 2:
                self.w("tensor", "dv", base_dv + h - 1)
            pb = t["PB"][:, wpar * 512:wpar * 512 + 256]
            self.P(lambda e, pb=pb, wpar=wpar, ksl=ksl: e.matmul(
                pb[:, 0:128], t["wk_s"][:, wpar * 128:(wpar + 1) * 128], ksl, start=True, stop=True))
            self.fire(self.P(lambda e, pb=pb, wpar=wpar, vsl=vsl: e.matmul(
                pb[:, 128:256], t["wk_s"][:, wpar * 128:(wpar + 1) * 128], vsl,
                start=True, stop=True)), "pe")
            self.w("vector", "pe", base_pe + h + 1 + (1 if h >= 2 else 0) * 0)
            self.fire(self.V(lambda e, h=h, pb=pb: e.scalar_tensor_tensor(
                out=t["S_HB"][:, h * 256:(h + 1) * 256], in0=t["S_HB"][:, h * 256:(h + 1) * 256],
                scalar=cf[:, COEF_DG + h:COEF_DG + h + 1], in1=pb[:, 0:256],
                op0=ALU.mult, op1=ALU.add)), "dv")
        if main:
            self.w("scalar", "dv")
            for h in range(16):
                ins = self.A(lambda e, h=h: e.activation(
                    out=t["Dbuf"][0][:, h * 64:(h + 1) * 64], in_=t["qt_b"][:, h * 64:(h + 1) * 64],
                    func=AF.Copy, scale=cf[:, COEF_TH + h:COEF_TH + h + 1]))
            self.fire(ins, "ac")
            self.w("vector", "ac")
            self.fire(self.V(lambda e: e.tensor_copy(t["Z_all"][:], t["Dbuf"][0][:])), "dv")
            self.w("tensor", "dv")
            for half in range(2):
                ins = self.P(lambda e, half=half: e.matmul(
                    t["PR"][:, half * 512:(half + 1) * 512], t["ident_b"][:],
                    t["qt_b"][:, half * 512:(half + 1) * 512], start=True, stop=False,
                    skip_group_check=True))
                self.fire(ins, "cp%d" % half)

    def chunk(self, q, n):
        t = self.t
        self.chunk_prep(q, n, main=True)
        cf = t["coef"][n % 2]
        base = {k: self.c[k].v for k in ("cp0", "cp1", "ca0", "ca1", "cv0", "cv1")}
        for i in range(NITER):
            for half in range(2):
                cp, ca, cv = "cp%d" % half, "ca%d" % half, "cv%d" % half
                if i == 0:
                    self.w("tensor", "ac")
                else:
                    self.w("tensor", cv, base[cv] + i)
                for h in range(half * 8, half * 8 + 8):
                    ins = self.P(lambda e, h=h, i=i: e.matmul(
                        t["PR"][:, h * 64:(h + 1) * 64], t["A_b"][:, h * 128:(h + 1) * 128],
                        t["Dbuf"][i % 2][:, h * 64:(h + 1) * 64], start=False, stop=(i == NITER - 1),
                        skip_group_check=True))
                self.fire(ins, cp)
                self.w("scalar", cp, base[cp] + i + 1)
                for h in range(half * 8, half * 8 + 8):
                    ins = self.A(lambda e, h=h, i=i: e.activation(
                        out=t["t_all"][:, h * 64:(h + 1) * 64], in_=t["PR"][:, h * 64:(h + 1) * 64],
                        func=AF.Copy, scale=cf[:, COEF_B + i * 16 + h:COEF_B + i * 16 + h + 1]))
                self.fire(ins, ca)
                self.w("vector", ca, base[ca] + i + 1)
                ins = None
                for h in range(half * 8, half * 8 + 8):
                    ins = self.V(lambda e, h=h, i=i: e.scalar_tensor_tensor(
                        out=t["Dbuf"][(i + 1) % 2][:, h * 64:(h + 1) * 64],
                        in0=t["Dbuf"][i % 2][:, h * 64:(h + 1) * 64],
                        scalar=cf[:, COEF_A + i * 16 + h:COEF_A + i * 16 + h + 1],
                        in1=t["t_all"][:, h * 64:(h + 1) * 64], op0=ALU.mult, op1=ALU.add)) or ins
                hs = half * 512
                self.fire(ins, cv)
                if i > 0:
                    self.fire(self.V(lambda e, hs=hs, i=i: e.tensor_add(
                        t["Z_all"][:, hs:hs + 512], t["Z_all"][:, hs:hs + 512],
                        t["Dbuf"][i % 2][:, hs:hs + 512])), "dv")
        self.w("vector", "cv0")
        self.w("vector", "cv1")
        self.fire(self.V(lambda e: e.tensor_add(
            t["Z_all"][:, 0:512], t["Z_all"][:, 0:512],
            t["Dbuf"][NITER % 2][:, 0:512])), "dv")
        self.w("tensor", "dv")
        self.w("tensor", "cp0")
        lastmm = None
        for h in range(16):
            kv = h // 2
            lastmm = self.P(lambda e, h=h, kv=kv: e.matmul(
                t["PB"][0:64, h * 64:(h + 1) * 64],
                t["kfm"][:, kv * QT + n * 64:kv * QT + (n + 1) * 64],
                t["qhat"][:, h * QT + n * 64:h * QT + (n + 1) * 64],
                start=True, stop=True)) or lastmm
        self.fire(self.P(lambda e: e.matmul(
            t["PX"][0:1, 448:449], t["ones_f"][:, 0:1], t["ones_f"][:, 0:1],
            start=True, stop=True)), "pe")
        self.w("vector", "pe")
        for half in range(2):
            self.fire(self.V(lambda e, half=half: e.tensor_mul(
                t["scm"][:, half * 512:(half + 1) * 512], t["PB"][0:64, half * 512:(half + 1) * 512],
                t["dmat"][:, half * 512:(half + 1) * 512])), "dv")
        self.fire(self.V(lambda e: e.tensor_add(
            t["Z_all"][:, 512:1024], t["Z_all"][:, 512:1024],
            t["Dbuf"][NITER % 2][:, 512:1024])), "dv")
        self.w("scalar", "dv")
        self.fire(self.A(lambda e: e.activation(out=t["Z_b"][:], in_=t["Z_all"][:],
                                                func=AF.Copy)), "ac")
        self.w("tensor", "ac")
        if q == 0 and n == NTAP:
            self.w("gpsimd", "cv0")
            self.w("gpsimd", "cv1")
            self.tap("ZA", lambda: t["Z_all"][:])
            self.tap("scmA", lambda: t["scm"][:])
            self.tap("cfA", lambda: t["coef"][1][:])
            self.tap("AbA", lambda: t["A_b"][:])
            self.tap("nBA", lambda: t["nB0"][:])
            self.tap("dmA", lambda: t["dmat"][:])
        self.w("tensor", "dv")
        for h in range(16):
            kv = h // 2
            pot = t["PO"] if h < 8 else t["PR"]
            pcol = (h % 8) * 128
            vsl = t["vtok"][:, (kv * NCH_Q + n) * 128:(kv * NCH_Q + n + 1) * 128]
            self.P(lambda e, h=h, pot=pot, pcol=pcol: e.matmul(
                pot[0:64, pcol:pcol + 128], t["Z_b"][:, h * 64:(h + 1) * 64],
                t["nB0"][:, h * 128:(h + 1) * 128],
                start=True, stop=False))
            self.fire(self.P(lambda e, h=h, pot=pot, pcol=pcol, vsl=vsl: e.matmul(
                pot[0:64, pcol:pcol + 128], t["scm"][:, h * 64:(h + 1) * 64],
                vsl, start=False, stop=True)), "pe")
        self.w("vector", "pe")
        for h in range(16):
            kv = h // 2
            pot = t["PO"] if h < 8 else t["PR"]
            pcol = (h % 8) * 128
            vsl = t["vtok"][:, (kv * NCH_Q + n) * 128:(kv * NCH_Q + n + 1) * 128]
            self.fire(self.V(lambda e, h=h, pot=pot, pcol=pcol, vsl=vsl: e.scalar_tensor_tensor(
                out=t["o1"][:, h * 128:(h + 1) * 128], in0=vsl,
                scalar=t["statsT"][:, n * 128 + 96 + h:n * 128 + 97 + h],
                in1=pot[0:64, pcol:pcol + 128], op0=ALU.mult, op1=ALU.add)), "dv")
        self.w("scalar", "dv")
        for qtr in range(4):
            for h in range(qtr * 4, qtr * 4 + 4):
                ins = self.A(lambda e, h=h, qtr=qtr: e.activation(
                    out=t["osq"][:, (h - qtr * 4) * 128:(h - qtr * 4 + 1) * 128],
                    in_=t["o1"][:, h * 128:(h + 1) * 128], func=AF.Square))
            self.fire(ins, "ac")
            self.w("vector", "ac")
            self.fire(self.V(lambda e, qtr=qtr: e.tensor_reduce(
                t["ms_c"][:, qtr * 4:(qtr + 1) * 4].unsqueeze(2),
                t["osq"][:, 0:512].rearrange("p (a b) -> p a b", b=128),
                mybir.AxisListType.X, ALU.add)), "dv")
            self.w("scalar", "dv")
            self.fire(self.A(lambda e, qtr=qtr: e.activation(
                out=t["rstd_c"][:, qtr * 4:(qtr + 1) * 4], in_=t["ms_c"][:, qtr * 4:(qtr + 1) * 4],
                func=AF.Sqrt, scale=t["kcc"][0:64, 5:6], bias=t["kcc"][0:64, 2:3])), "ac")
            self.w("vector", "ac")
            self.fire(self.V(lambda e, qtr=qtr: e.reciprocal(
                t["rstd_c"][:, qtr * 4:(qtr + 1) * 4],
                t["rstd_c"][:, qtr * 4:(qtr + 1) * 4])), "dv")
        if q == 0 and n == NTAP:
            self.tap("o1A", lambda: t["o1"][:])
            self.tap("msA", lambda: t["ms_c"][:])
            self.tap("rsA", lambda: t["rstd_c"][:])
        pp = (n % 2) * 64
        for h in range(16):
            self.fire(self.V(lambda e, h=h, pp=pp: e.tensor_scalar(
                t["opair"][pp:pp + 64, h * 128:(h + 1) * 128], t["o1"][:, h * 128:(h + 1) * 128],
                t["rstd_c"][:, h:h + 1], None, ALU.mult)), "dv")
        if n % 2 == 1:
            ptb = t["PX"][:, 320:384].bitcast(BF16)
            pair = n // 2
            if q == 0 and (n == NTAP or (NTAP % 2 == 0 and n == NTAP + 1)) and n % 2 == 1:
                self.tap("opA", lambda: t["opair"][:])
            for h in range(16):
                self.w("tensor", "dv")
                self.w("tensor", "ac")
                self.P(lambda e, h=h: e.transpose(
                    ptb, t["opair"][:, h * 128:(h + 1) * 128], t["ident_b"][:]))
                self.fire(self.P(lambda e: e.matmul(
                    t["PX"][0:1, 448:449], t["ones_f"][:, 0:1], t["ones_f"][:, 0:1],
                    start=True, stop=True)), "pe")
                self.w("vector", "pe")
                self.V(lambda e, h=h, pair=pair: e.tensor_mul(
                    t["of_f"][:], ptb,
                    t["gsil"][:, h * QT + pair * 128:h * QT + (pair + 1) * 128]))
                self.fire(self.V(lambda e, h=h, pair=pair: e.tensor_scalar(
                    t["ofin"][:, h * QT + pair * 128:h * QT + (pair + 1) * 128],
                    t["of_f"][:], t["nw_col"][:, 0:1], None, ALU.mult)), "dv")
            if q == 0 and (n == NTAP or (NTAP % 2 == 0 and n == NTAP + 1)) and n % 2 == 1:
                self.tap("ofA", lambda: t["ofin"][:])

    def chunk_scan_only(self, q, n):
        self.chunk_prep(q, n, main=False)

    def wo_quarter(self, q):
        t = self.t
        for who in ("tensor", "sync", "gpsimd"):
            self.w(who, "dv")
        ncol = [0]
        for ot in range(KT):
            par = self.stream_col(t["wo_t"][ot], ot, ncol[0])
            ncol[0] += 1
            self.w("tensor", "dv")
            lastmm = None
            for kt in range(KT):
                lastmm = self.P(lambda e, kt=kt, par=par: e.matmul(
                    t["PC"][:, 0:QT], t["wst_b"][par][:, kt * 128:(kt + 1) * 128],
                    t["ofin"][:, kt * QT:(kt + 1) * QT], start=(kt == 0), stop=(kt == KT - 1))) or lastmm
            self.fire(lastmm, "pe")
            xpar = ot % 2
            if ot >= 2:
                self.w("sync", "sp")
            self.fire(self.S(lambda e, ot=ot, xpar=xpar: e.dma_start(
                out=t["xq"][xpar][:], in_=t["x_t"][ot * 128:(ot + 1) * 128, 3 + q * QT:3 + (q + 1) * QT])), "sx", 16)
            self.w("vector", "pe")
            self.w("vector", "sx")
            self.fire(self.V(lambda e, xpar=xpar, ot=ot: e.tensor_add(
                t["x1q"][xpar][:], t["PC"][:, 0:QT], t["xq"][xpar][:])), "dv")
            self.w("gpsimd", "dv")
            self.fire(self.G(lambda e, ot=ot, xpar=xpar: e.dma_start(
                out=t["x1sp"][ot * 128:(ot + 1) * 128, q * QT:(q + 1) * QT], in_=t["x1q"][xpar][:])), "sp", 16)
            self.w("scalar", "dv")
            self.fire(self.A(lambda e, xpar=xpar: e.activation(out=t["sqt"][:], in_=t["x1q"][xpar][:],
                                                               func=AF.Square)), "ac")
            self.w("tensor", "ac")
            self.w("tensor", "dv", self.c["dv"].v)
            self.P(lambda e, ot=ot, xpar=xpar: e.matmul(t["PX"][0:1, 0:QT], t["ones_c"][:],
                                                        t["x1q"][xpar][:], start=(ot == 0), stop=(ot == KT - 1)))
            self.fire(self.P(lambda e, ot=ot: e.matmul(t["PB"][0:1, 0:QT], t["ones_c"][:],
                                                       t["sqt"][:], start=(ot == 0), stop=(ot == KT - 1))), "pe")
        self.w("scalar", "pe")
        self.A(lambda e, q=q: e.activation(out=t["m2_row"][:, q * QT:(q + 1) * QT], in_=t["PX"][0:1, 0:QT],
                                           func=AF.Copy, scale=t["kcc"][0:1, 3:4]))
        self.fire(self.A(lambda e, q=q: e.activation(out=t["sq2_row"][:, q * QT:(q + 1) * QT],
                                                     in_=t["PB"][0:1, 0:QT], func=AF.Copy, scale=t["kcc"][0:1, 3:4])), "ac")
        for who in ("tensor", "vector", "gpsimd", "sync"):
            self.w(who, "ac")

    def ffn(self):
        t = self.t
        self.w("scalar", "ac", self.c["ac"].v)
        self.fire(self.A(lambda e: e.activation(out=t["rs2_row"][:], in_=t["m2_row"][:], func=AF.Square)), "ac")
        self.w("vector", "ac")
        self.fire(self.V(lambda e: e.tensor_sub(t["sq2_row"][:], t["sq2_row"][:], t["rs2_row"][:])), "dv")
        self.w("scalar", "dv")
        self.fire(self.A(lambda e: e.activation(out=t["rs2_row"][:], in_=t["sq2_row"][:],
                                                func=AF.Sqrt, bias=t["kcc"][0:1, 0:1])), "ac")
        self.w("vector", "ac")
        self.fire(self.V(lambda e: e.reciprocal(t["rs2_row"][:], t["rs2_row"][:])), "dv")
        self.w("tensor", "dv")
        for g in range(4):
            self.w("tensor", "ac")
            self.P(lambda e, g=g: e.matmul(t["PB"][:, 0:512], t["ones_f"][:],
                                           t["m2_row"][:, g * 512:(g + 1) * 512], start=True, stop=True))
            self.fire(self.P(lambda e, g=g: e.matmul(t["PB"][:, 512:1024], t["ones_f"][:],
                                                     t["rs2_row"][:, g * 512:(g + 1) * 512],
                                                     start=True, stop=True)), "pe")
            self.w("scalar", "pe")
            self.A(lambda e, g=g: e.activation(out=t["m2_bc"][:, g * 512:(g + 1) * 512],
                                               in_=t["PB"][:, 0:512], func=AF.Copy))
            self.fire(self.A(lambda e, g=g: e.activation(out=t["rs2_bc"][:, g * 512:(g + 1) * 512],
                                                         in_=t["PB"][:, 512:1024], func=AF.Copy)), "ac")
        self.fire(self.S(lambda e: e.dma_start(out=t["gaf"][:], in_=t["ga_tl"])), "sx", 16)
        self.fire(self.S(lambda e: e.dma_start(out=t["gbf"][:], in_=t["gb_tl"])), "sx", 16)
        self.fire(self.S(lambda e: e.dma_start(
            out=t["gwf"][:], in_=t["gw_tl"])), "sx", 16)
        self.w("gpsimd", "sx")
        self.fire(self.G(lambda e: e.tensor_scalar_mul(t["gbf2"][:], t["gbf"][:], 2.0)), "po")
        self.w("tensor", "po")
        self.w("tensor", "sx")
        for kt in range(KT):
            px = 8 * (kt % 2)
            if kt >= 2:
                self.w("tensor", "dv")
            self.fire(self.P(lambda e, kt=kt, px=px: e.matmul(
                t["PX"][0:128, px:px + 8], t["gaf"][:, kt * 128:(kt + 1) * 128],
                t["gbf2"][:], start=True, stop=True)), "pe")
            self.w("vector", "pe")
            self.fire(self.V(lambda e, kt=kt, px=px: e.tensor_add(
                t["wrt32"][:, kt * 8:(kt + 1) * 8], t["PX"][0:128, px:px + 8],
                t["gwf"][:, kt * 8:(kt + 1) * 8])), "dv")
        self.w("tensor", "dv")
        for kt in range(KT):
            ins = self.P(lambda e, kt=kt: e.matmul(
                t["PX"][0:8, 32:33], t["wrt32"][:, kt * 8:(kt + 1) * 8],
                t["ones_c"][:, 0:1], start=(kt == 0), stop=(kt == KT - 1)))
        self.fire(ins, "pe")
        self.fire(self.P(lambda e: e.matmul(
            t["PX"][0:8, 34:35], t["wrt32"][:, 0:8],
            t["ones_c"][:, 0:1], start=True, stop=True)), "pe")
        self.w("scalar", "pe")
        self.fire(self.A(lambda e: e.activation(out=t["ncsW"][:], in_=t["PX"][0:8, 32:33],
                                                func=AF.Copy, scale=t["kcc"][0:8, 4:5])), "ac")
        self.w("tensor", "ac")
        for dt in range(KT):
            for hh in range(2):
                par = (2 * dt + hh) % 2
                if 2 * dt + hh >= 2:
                    self.w("sync", "po")
                    self.w("sync", "pe")
                self.fire(self.S(lambda e, dt=dt, hh=hh, par=par: e.dma_start(
                    out=t["wst_f"][par][:], in_=t["x1sp"][dt * 128:(dt + 1) * 128,
                                                          hh * 1024:(hh + 1) * 1024])), "sx", 16)
                self.w("gpsimd", "sx")
                self.w("gpsimd", "ac")
                self.G(lambda e, par=par, hh=hh: e.tensor_sub(
                    t["xfw2"][:, 0:1024], t["wst_f"][par][:], t["m2_bc"][:, hh * 1024:(hh + 1) * 1024]))
                self.fire(self.G(lambda e, dt=dt, hh=hh: e.tensor_mul(
                    t["xn2"][:, dt * T_HALF + hh * 1024:dt * T_HALF + (hh + 1) * 1024],
                    t["xfw2"][:, 0:1024], t["rs2_bc"][:, hh * 1024:(hh + 1) * 1024])), "po")
                self.w("tensor", "sx")
                accT = t["PB"] if hh == 0 else t["PR"]
                self.P(lambda e, dt=dt, par=par, accT=accT: e.matmul(
                    accT[0:8, 0:512], t["wrt32"][:, dt * 8:(dt + 1) * 8],
                    t["wst_f"][par][:, 0:512], start=(dt == 0), stop=(dt == KT - 1)))
                self.fire(self.P(lambda e, dt=dt, par=par, accT=accT: e.matmul(
                    accT[0:8, 512:1024], t["wrt32"][:, dt * 8:(dt + 1) * 8],
                    t["wst_f"][par][:, 512:1024], start=(dt == 0), stop=(dt == KT - 1))), "pe")
        for who in ("tensor", "sync", "vector", "scalar"):
            self.w(who, "po")
        self.w("gpsimd", "po")
        self.tap("x2d", lambda: t["xn2"][:])
        if FPARTS < 2:
            self.finish_early()
            return
        for th in range(2):
            accT = t["PB"] if th == 0 else t["PR"]
            self.w("tensor", "pe")
            self.w("tensor", "dv")
            self.w("scalar", "pe")
            self.fire(self.A(lambda e, th=th, accT=accT: e.activation(
                out=t["lgT"][:, th * 1024:(th + 1) * 1024], in_=accT[0:8, 0:1024],
                func=AF.Copy)), "ac")
            self.P(lambda e, th=th: e.matmul(
                t["PO"][0:8, 0:512], t["ones_f"][0:1, 0:8],
                t["m2_row"][0:1, th * 1024:th * 1024 + 512], start=True, stop=True))
            self.fire(self.P(lambda e, th=th: e.matmul(
                t["PO"][0:8, 512:1024], t["ones_f"][0:1, 0:8],
                t["m2_row"][0:1, th * 1024 + 512:(th + 1) * 1024], start=True, stop=True)), "pe")
            self.w("vector", "pe")
            self.w("vector", "ac")
            self.fire(self.V(lambda e, th=th: e.scalar_tensor_tensor(
                out=t["lgT"][:, th * 1024:(th + 1) * 1024], in0=t["PO"][0:8, 0:1024],
                scalar=t["ncsW"][0:8, 0:1], in1=t["lgT"][:, th * 1024:(th + 1) * 1024],
                op0=ALU.mult, op1=ALU.add)), "dv")
            self.w("tensor", "dv")
            self.P(lambda e, th=th: e.matmul(
                t["PO"][0:8, 0:512], t["ones_f"][0:1, 0:8],
                t["rs2_row"][0:1, th * 1024:th * 1024 + 512], start=True, stop=True))
            self.fire(self.P(lambda e, th=th: e.matmul(
                t["PO"][0:8, 512:1024], t["ones_f"][0:1, 0:8],
                t["rs2_row"][0:1, th * 1024 + 512:(th + 1) * 1024], start=True, stop=True)), "pe")
            self.w("vector", "pe")
            self.fire(self.V(lambda e, th=th: e.tensor_mul(
                t["lgT"][:, th * 1024:(th + 1) * 1024], t["lgT"][:, th * 1024:(th + 1) * 1024],
                t["PO"][0:8, 0:1024])), "dv")
        for tt in range(16):
            self.w("tensor", "dv")
            self.w("tensor", "ac")
            self.P(lambda e, tt=tt: e.transpose(
                t["PX"][0:128, 0:8], t["lgT"][:, tt * 128:(tt + 1) * 128], t["ident_f"][0:8, 0:8]))
            self.fire(self.P(lambda e: e.matmul(
                t["PX"][0:1, 448:449], t["ones_f"][:, 0:1], t["ones_f"][:, 0:1],
                start=True, stop=True)), "pe")
            self.w("vector", "pe")
            self.fire(self.V(lambda e, tt=tt: e.tensor_copy(
                t["lgt"][:, tt * 8:(tt + 1) * 8], t["PX"][0:128, 0:8])), "dv")
        fc = t["fcol"]
        E8 = lambda tn, tt: t[tn][:, tt * 8:(tt + 1) * 8]
        for tt in range(16):
            self.fire(self.V(lambda e, tt=tt: e.reduce_max(
                fc[:, tt:tt + 1], t["lgt"][:, tt * 8:(tt + 1) * 8], mybir.AxisListType.X)), "dv")
        for tt in range(16):
            self.fire(self.V(lambda e, tt=tt: e.tensor_scalar(
                E8("pr8", tt), t["lgt"][:, tt * 8:(tt + 1) * 8], fc[:, tt:tt + 1],
                None, ALU.subtract)), "dv")
        self.w("scalar", "dv")
        self.fire(self.A(lambda e: e.activation(out=t["pr8"][:], in_=t["pr8"][:], func=AF.Exp)), "ac")
        self.w("vector", "ac")
        for tt in range(16):
            self.fire(self.V(lambda e, tt=tt: e.reduce_max(
                fc[:, 48 + tt:49 + tt], E8("pr8", tt), mybir.AxisListType.X)), "dv")
        for tt in range(16):
            self.fire(self.V(lambda e, tt=tt: e.tensor_scalar(
                E8("is1", tt), E8("pr8", tt), fc[:, 48 + tt:49 + tt], None, ALU.is_ge)), "dv")
        for tt in range(16):
            self.fire(self.V(lambda e, tt=tt: e.scalar_tensor_tensor(
                out=E8("p2", tt), in0=E8("is1", tt), scalar=-1e9, in1=E8("pr8", tt),
                op0=ALU.mult, op1=ALU.add)), "dv")
        for tt in range(16):
            self.fire(self.V(lambda e, tt=tt: e.reduce_max(
                fc[:, 64 + tt:65 + tt], E8("p2", tt), mybir.AxisListType.X)), "dv")
        for tt in range(16):
            self.fire(self.V(lambda e, tt=tt: e.tensor_scalar(
                E8("is1", tt), E8("pr8", tt), fc[:, 64 + tt:65 + tt], None, ALU.is_ge)), "dv")
        self.fire(self.V(lambda e: e.tensor_add(fc[:, 48:64], fc[:, 48:64], fc[:, 64:80])), "dv")
        for tt in range(8):
            self.fire(self.V(lambda e, tt=tt: e.tensor_mul(
                E8("pr8", tt), E8("pr8", tt), E8("is1", tt))), "dv")
        self.fire(self.V(lambda e: e.reciprocal(fc[:, 80:96], fc[:, 48:64])), "dv")
        for tt in range(8, 16):
            self.fire(self.V(lambda e, tt=tt: e.tensor_mul(
                E8("pr8", tt), E8("pr8", tt), E8("is1", tt))), "dv")
        for tt in range(16):
            self.fire(self.V(lambda e, tt=tt: e.tensor_scalar(
                t["wmask"][:, tt * 8:(tt + 1) * 8], E8("pr8", tt), fc[:, 80 + tt:81 + tt],
                None, ALU.mult)), "dv")
        if FPARTS < 3:
            self.finish_early()
            return
        self.tap("wmk", lambda: t["wmask"][:])
        self.tap("pr8d", lambda: t["pr8"][:])
        self.tap("fcd", lambda: t["fcol"][:])
        self.tap("is1d", lambda: t["is1"][:])
        for hh in range(2):
            self.fire(self.S(lambda e, hh=hh: e.dma_start(
                out=t["wst_f"][hh][:],
                in_=t["gupA_tl"][:, hh * 1024:(hh + 1) * 1024])), "sx", 16)
        self.w("gpsimd", "sx")
        for hh in range(2):
            self.fire(self.G(lambda e, hh=hh: e.tensor_copy(
                t["galla_b"][:, hh * 1024:(hh + 1) * 1024], t["wst_f"][hh][:])), "po")
        self.w("tensor", "po")
        for tg in range(4):
            self.w("tensor", "dv")
            for kt in range(KT):
                ins = self.P(lambda e, tg=tg, kt=kt: e.matmul(
                    t["PB"][:, 0:512], t["galla_b"][:, kt * 128:(kt + 1) * 128],
                    t["xn2"][:, kt * T_HALF + tg * 512:kt * T_HALF + (tg + 1) * 512],
                    start=(kt == 0), stop=(kt == KT - 1)))
            self.fire(ins, "pe")
            self.w("vector", "pe")
            self.fire(self.V(lambda e: e.tensor_copy(t["t1b"][:], t["PB"][:, 0:512])), "dv")
            self.w("gpsimd", "dv")
            self.fire(self.G(lambda g, tg=tg: g.dma_start(
                out=t["t1sp"][:, tg * 512:(tg + 1) * 512], in_=t["t1b"][:])), "sp", 16)
        for who in ("tensor", "scalar", "vector", "sync"):
            self.w(who, "sp")
        if FPARTS < 4:
            self.finish_early()
            return
        nc_ctr = [0]
        for e_ in range(E_EXP):
            self.w("sync", "pe")
            self.fire(self.S(lambda e, e_=e_: e.dma_start(
                out=t["t1r_e"][:], in_=t["t1sp"][e_ * 16:(e_ + 1) * 16, :])), "sx", 16)
            for tt in range(16):
                self.w("tensor", "dv")
                self.w("tensor", "sp")
                self.P(lambda e, tt=tt: e.transpose(
                    t["PX"][0:8, 0:128], t["wmask"][:, tt * 8:(tt + 1) * 8], t["ident_f"][:]))
                self.fire(self.P(lambda e: e.matmul(
                    t["PX"][0:1, 448:449], t["ones_f"][:, 0:1], t["ones_f"][:, 0:1],
                    start=True, stop=True)), "pe")
                self.w("vector", "pe")
                self.fire(self.V(lambda e: e.tensor_copy(t["w8"][:], t["PX"][0:8, 0:128])), "dv")
                self.w("gpsimd", "dv")
                self.fire(self.G(lambda g, e_=e_, tt=tt: g.dma_start(
                    out=t["wrow"][0:1, tt * 128:(tt + 1) * 128],
                    in_=t["w8"][e_:e_ + 1, :])), "sp", 16)
            self.w("tensor", "sp", self.c["sp"].v)
            for g in range(4):
                gb = (g % 2) * 512
                self.w("tensor", "ac")
                self.fire(self.P(lambda e, g=g, gb=gb: e.matmul(
                    t["PB"][:, gb:gb + 512], t["ones_f"][:],
                    t["wrow"][:, g * 512:(g + 1) * 512],
                    start=True, stop=True)), "pe")
                self.w("scalar", "pe")
                self.fire(self.A(lambda e, g=g, gb=gb: e.activation(
                    out=t["wbc"][0][:, g * 512:(g + 1) * 512], in_=t["PB"][:, gb:gb + 512], func=AF.Copy)), "ac")
            parB = self.stream_load_small(t["gupB_tl"][e_], t["gupB_b"], 2.0, 16, nc_ctr)
            self.w("tensor", "po")
            for i8 in range(8):
                parg = self.stream_col(t["gup_t"][e_][i8], i8, nc_ctr[0]); nc_ctr[0] += 1
                for tg in range(4):
                    self.w("tensor", "ac")
                    self.w("tensor", "dv")
                    for kt in range(KT):
                        self.P(lambda e, kt=kt, tg=tg, parg=parg: e.matmul(
                            t["PC"][:, 0:512], t["wst_b"][parg][:, kt * 128:(kt + 1) * 128],
                            t["xn2"][:, kt * T_HALF + tg * 512:kt * T_HALF + (tg + 1) * 512],
                            start=(kt == 0), stop=False))
                    self.fire(self.P(lambda e, i8=i8, tg=tg: e.matmul(
                        t["PC"][:, 0:512], t["gupB_b"][:, i8 * 128:(i8 + 1) * 128],
                        t["t1r_e"][:, tg * 512:(tg + 1) * 512],
                        start=False, stop=True)), "pe")
                    self.w("scalar", "pe")
                    self.fire(self.A(lambda e, tg=tg: e.activation(
                        out=t["sil_s"][:], in_=t["PC"][:, 0:512], func=AF.Silu)), "ac")
                    self.fire(self.A(lambda e, tg=tg: e.activation(
                        out=t["silq"][:, tg * 512:(tg + 1) * 512], in_=t["sil_s"][:], func=AF.Copy)), "ac")
                paru = self.stream_col(t["gup_t"][e_][i8 + 8], i8 + 8, nc_ctr[0]); nc_ctr[0] += 1
                for tg in range(4):
                    self.w("tensor", "ac")
                    self.w("tensor", "dv")
                    for kt in range(KT):
                        self.P(lambda e, kt=kt, tg=tg, paru=paru: e.matmul(
                            t["PC"][:, 0:512], t["wst_b"][paru][:, kt * 128:(kt + 1) * 128],
                            t["xn2"][:, kt * T_HALF + tg * 512:kt * T_HALF + (tg + 1) * 512],
                            start=(kt == 0), stop=False))
                    self.fire(self.P(lambda e, i8=i8, tg=tg: e.matmul(
                        t["PC"][:, 0:512], t["gupB_b"][:, (i8 + 8) * 128:(i8 + 9) * 128],
                        t["t1r_e"][:, tg * 512:(tg + 1) * 512],
                        start=False, stop=True)), "pe")
                    self.w("vector", "pe")
                    self.w("vector", "ac")
                    self.V(lambda e, tg=tg: e.tensor_mul(t["htf"][:], t["silq"][:, tg * 512:(tg + 1) * 512],
                                                         t["PC"][:, 0:512]))
                    self.fire(self.V(lambda e, tg=tg: e.tensor_mul(
                        t["ht_b"][:], t["htf"][:], t["wbc"][0][:, tg * 512:(tg + 1) * 512])), "dv")
                    self.w("gpsimd", "dv")
                    self.fire(self.G(lambda g, e_=e_, i8=i8, tg=tg: g.dma_start(
                        out=t["hsp"][e_, i8, :, tg * 512:(tg + 1) * 512], in_=t["ht_b"][:])), "sp", 16)
        for who in ENGINES:
            self.w(who, "sp")
        for i8 in range(8):
            self.tap("hspd%d" % i8, lambda i8=i8: t["hsp"][0, i8])
        if FPARTS < 5:
            self.finish_early()
            return
        self.pass_b()

    def pass_b(self):
        t = self.t
        for dt in range(KT):
            self.fire(self.S(lambda e, dt=dt: e.dma_start(
                out=t["yacc"][:, dt * T_HALF:(dt + 1) * T_HALF],
                in_=t["x1sp"][dt * 128:(dt + 1) * 128, :])), "sx", 16)
        for who in ("vector", "tensor", "gpsimd"):
            self.w(who, "sx")
        ncd = [0]
        for e_ in range(E_EXP):
            self.w("sync", "pe")
            for i8 in range(8):
                self.fire(self.S(lambda e, e_=e_, i8=i8: e.dma_start(
                    out=t["hload"][:, i8 * T_HALF:(i8 + 1) * T_HALF], in_=t["hsp"][e_, i8])), "sx", 16)
            self.fire(self.S(lambda e, e_=e_: e.dma_start(
                out=t["dwAf"][:], in_=t["dwnA_tl"][e_])), "sx", 16)
            parB = self.stream_load_small(t["dwnB_tl"][e_], t["dwnB_b"], 2.0, 16, ncd)
            self.w("gpsimd", "sx")
            self.fire(self.G(lambda e: e.tensor_copy(t["dwnA_b"][:], t["dwAf"][:])), "po")
            self.w("tensor", "po")
            for tg in range(4):
                self.w("tensor", "dv")
                for kt in range(8):
                    ins = self.P(lambda e, kt=kt, tg=tg: e.matmul(
                        t["PB"][0:16, 0:512],
                        t["dwnA_b"][:, kt * 16:(kt + 1) * 16],
                        t["hload"][:, kt * T_HALF + tg * 512:kt * T_HALF + (tg + 1) * 512],
                        start=(kt == 0), stop=(kt == 7)))
                self.fire(ins, "pe")
                self.w("vector", "pe")
                self.fire(self.V(lambda e, tg=tg: e.tensor_copy(
                    t["t3r"][:, tg * 512:(tg + 1) * 512], t["PB"][0:16, 0:512])), "dv")
            self.w("tensor", "dv")
            for ot in range(KT):
                par = self.stream_col(t["dwn_t"][e_][ot], ot, ncd[0]); ncd[0] += 1
                for tg in range(4):
                    self.w("tensor", "dv")
                    for kt in range(8):
                        self.P(lambda e, kt=kt, tg=tg, par=par: e.matmul(
                            t["PC"][:, 0:512], t["wst_b"][par][:, kt * 128:(kt + 1) * 128],
                            t["hload"][:, kt * T_HALF + tg * 512:kt * T_HALF + (tg + 1) * 512],
                            start=(kt == 0), stop=False))
                    self.fire(self.P(lambda e, ot=ot, tg=tg: e.matmul(
                        t["PC"][:, 0:512], t["dwnB_b"][:, ot * 128:(ot + 1) * 128],
                        t["t3r"][:, tg * 512:(tg + 1) * 512], start=False, stop=True)), "pe")
                    self.w("vector", "pe")
                    ysl = t["yacc"][:, ot * T_HALF + tg * 512:ot * T_HALF + (tg + 1) * 512]
                    self.fire(self.V(lambda e, ysl=ysl: e.tensor_add(ysl, ysl, t["PC"][:, 0:512])), "dv")
        for who in ("gpsimd",):
            self.w(who, "dv")
        for dt in range(KT):
            self.fire(self.G(lambda e, dt=dt: e.dma_start(
                out=t["out_t"][dt * 128:(dt + 1) * 128, :],
                in_=t["yacc"][:, dt * T_HALF:(dt + 1) * T_HALF])), "so", 16)
        self.w("gpsimd", "so")


class Ctr:
    __slots__ = ("sem", "v")

    def __init__(self, s):
        self.sem = s
        self.v = 0


def build_nc(taps_cfg=None):
    taps_cfg = taps_cfg or {}
    nc = bass.Bass("TRN2", target_bir_lowering=False)
    t = {}

    def din(n, s, dt=F32):
        t[n] = nc.dram_tensor(n, s, dt, kind="ExternalInput").ap()

    din("x_t", [D, 3 + T_HALF])
    din("xp_t", [D, 3 + T_HALF])
    din("pmask", [128, 1])
    din("wq_t", [KT, 128, D])
    din("wk_t", [KV_OT, 128, D])
    din("wv_t", [KV_OT, 128, D])
    din("wg_t", [KT, 128, D])
    din("wo_t", [KT, 128, D])
    din("wsm_t", [1, 128, D])
    din("convq", [128, KT * CONV])
    din("convk", [128, KV_OT * CONV])
    din("convv", [128, KV_OT * CONV])
    din("svec", [128, 2])
    din("nwv", [128, 1])
    din("gup_t", [E_EXP, KT, 128, D])
    din("dwn_t", [E_EXP, KT, 128, 8 * 128])
    din("gw_tl", [128, KT * E_EXP])
    din("ga_tl", [R_LORA, D])
    din("gb_tl", [R_LORA, E_EXP])
    din("gupA_tl", [128, KT * 128])
    din("gupB_tl", [E_EXP, R_LORA, D])
    din("dwnA_tl", [E_EXP, 128, 8 * R_LORA])
    din("dwnB_tl", [E_EXP, R_LORA, D])
    t["out_t"] = nc.dram_tensor("out_t", [D, T_HALF], F32, kind="ExternalOutput").ap()
    t["x1sp"] = nc.dram_tensor("x1sp", [D, T_HALF], F32, kind="Internal").ap()
    t["hsp"] = nc.dram_tensor("hsp", [E_EXP, 8, 128, T_HALF], BF16, kind="Internal").ap()
    t["t1sp"] = nc.dram_tensor("t1sp", [128, T_HALF], BF16, kind="Internal").ap()
    taps = {}
    for tn, cfg in taps_cfg.items():
        shape, tdt = cfg if isinstance(cfg, tuple) else (cfg, F32)
        taps[tn] = nc.dram_tensor("tap_" + tn, shape, tdt, kind="ExternalOutput").ap()

    with ExitStack() as ctx:
        def sbt(n, s, dt=F32):
            t[n] = ctx.enter_context(nc.sbuf_tensor(n, s, dt))

        def pst(n, s, dt=F32):
            t[n] = ctx.enter_context(nc.psum_tensor(n, s, dt))

        sbt("ident_b", [128, 128], BF16)
        sbt("ident_f", [128, 128], F32)

        sbt("ones_f", [1, 128], F32)
        sbt("ones_c", [128, 1], F32)
        sbt("kcc", [128, 8], F32)
        sbt("ones64", [64, 1], F32)
        sbt("nw_col", [128, 1], F32)

        sbt("pm_col", [128, 1], F32)

        sbt("wst_f0", [128, D // 2], F32)
        sbt("wst_f1", [128, D // 2], F32)
        sbt("wst_b0", [128, D], BF16)
        sbt("wst_b1", [128, D], BF16)
        t["wst_f"] = [t["wst_f0"], t["wst_f1"]]
        t["wst_b"] = [t["wst_b0"], t["wst_b1"]]


        mctx = ExitStack()
        for _nm in ("m2_row", "sq2_row", "rs2_row"):
            t[_nm] = mctx.enter_context(nc.sbuf_tensor(_nm, [1, T_HALF], F32))

        with ExitStack() as actx:
            def sba(n, s, dt=F32):
                t[n] = actx.enter_context(nc.sbuf_tensor(n, s, dt))

            sba("S_HB", [128, HQ * 256], F32)
            sba("mask64", [64, 64], F32)
            sba("sv_col", [128, 2], F32)
            sba("nega", [128, 2], F32)
            sba("cvq_c", [128, KT * CONV], F32)
            sba("cvk_c", [128, KV_OT * CONV], F32)
            sba("cvv_c", [128, KV_OT * CONV], F32)
            sba("wsm_b", [128, KT * 128], BF16)
            sba("xq0", [128, QT], F32)
            sba("xq1", [128, QT], F32)
            t["xq"] = [t["xq0"], t["xq1"]]
            sba("sqt", [128, QT], F32)
            sba("A_b", [128, HQ * 128], BF16)
            sba("nB0", [128, HQ * 128], BF16)
            sba("qt_b", [128, HQ * 64], BF16)
            sba("Db0", [128, HQ * 64], BF16)
            sba("Db1", [128, HQ * 64], BF16)
            t["Dbuf"] = [t["Db0"], t["Db1"]]
            sba("t_all", [128, HQ * 64], BF16)
            sba("Z_all", [128, HQ * 64], F32)
            sba("Z_b", [128, HQ * 64], BF16)
            sba("coef0", [128, COEF_W], F32)
            sba("coef1", [128, COEF_W], F32)
            t["coef"] = [t["coef0"], t["coef1"]]
            sba("glog_t", [16, QT], F32)
            sba("beta_t", [16, QT], F32)
            sba("alpha_t", [16, QT], F32)
            sba("cs_t", [16, QT], F32)
            sba("wend_t", [16, QT], F32)
            sba("statsT", [64, NCH_Q * 128], F32)
            sba("ksq_t", [64, KV_OT], F32)
            sba("trv_t", [64, 16], F32)
            sba("dgq_row", [1, W_CH], F32)
            sba("trdh_row", [1, 16], F32)
            sba("tr_row", [1, 16], F32)
            sba("tr_tmp", [1, 16], F32)
            sba("lmax_row", [1, W_CH], F32)
            sba("csr", [1, 16 * QT], F32)
            sba("th_row", [1, W_CH], F32)
            sba("dl_row", [1, W_CH], F32)
            sba("s1_row", [1, W_CH], F32)
            sba("rho0", [1, W_CH], F32)
            sba("rho1", [1, W_CH], F32)
            t["rho_r"] = [t["rho0"], t["rho1"]]
            sba("a_tab", [1, NITER * W_CH], F32)
            sba("b_tab", [1, NITER * W_CH], F32)
            sba("th_tab", [1, W_CH], F32)
            sba("xn1", [128, KT * QT], BF16)
            sba("xnh", [128, KT * 3], BF16)
            sba("xh_t", [128, KT * 3], F32)
            sba("xh_f", [128, KT * 3], F32)
            sba("sqh", [128, KT * 3], F32)
            sba("qhat", [128, HQ * QT], BF16)
            sba("gsil", [128, HQ * QT], BF16)
            sba("kfm", [128, KV_OT * QT], BF16)
            sba("ktok", [64, KV_OT * NCH_Q * 128], BF16)
            sba("vtok", [64, KV_OT * NCH_Q * 128], BF16)
            sba("vfm_s", [128, QT], BF16)
            sba("wk_s", [64, 256], BF16)
            sba("zbuf", [128, 3 + QT + 1], F32)
            t["comb_t"] = t["zbuf"]
            sba("zacc", [128, QT], F32)
            sba("ztails", [128, (KT + 2 * KV_OT) * 3], F32)
            sba("spx", [128, QT], F32)
            sba("m_row", [1, QT], F32)
            sba("sq_row", [1, QT], F32)
            sba("rs_row", [1, QT], F32)
            sba("mh_row", [1, 4], F32)
            sba("rsh_row", [1, 4], F32)
            sba("vh_row", [1, 4], F32)
            sba("m_bc", [128, QT], F32)
            sba("rs_bc", [128, QT], F32)
            sba("xfw", [128, QT], F32)
            sba("dmat", [64, HQ * 64], F32)
            sba("scm", [64, HQ * 64], BF16)
            sba("o1", [64, HQ * 128], BF16)
            sba("osq", [64, 512], F32)
            sba("ms_c", [64, 16], F32)
            sba("rstd_c", [64, 16], F32)
            sba("opair", [128, HQ * 128], BF16)
            sba("of_f", [128, 128], F32)
            t["x1q"] = [t["spx"], t["zacc"]]
            t["ofin"] = t["xn1"]
            ATTN_KEYS = None

        with ExitStack() as fctx:
            def sbf(n, s, dt=F32):
                t[n] = fctx.enter_context(nc.sbuf_tensor(n, s, dt))

            sbf("xn2", [128, KT * T_HALF], BF16)
            sbf("m2_bc", [128, T_HALF], BF16)
            sbf("rs2_bc", [128, T_HALF], BF16)
            sbf("xfw2", [128, 1024], F32)
            sbf("gaf", [16, D], F32)
            sbf("gbf", [16, E_EXP], F32)
            sbf("gbf2", [16, E_EXP], F32)
            sbf("gwf", [128, KT * E_EXP], F32)
            sbf("wrt32", [128, KT * E_EXP], F32)
            sbf("ncsW", [8, 1], F32)
            sbf("lgT", [8, T_HALF], F32)
            sbf("lgt", [128, KT * E_EXP], F32)
            sbf("fcol", [128, 96], F32)
            sbf("pr8", [128, 16 * E_EXP], F32)
            sbf("is1", [128, 16 * E_EXP], F32)
            sbf("p2", [128, 16 * E_EXP], F32)
            sbf("wmask", [128, 16 * E_EXP], F32)
            sbf("w8", [8, 128], F32)
            sbf("wrow", [1, T_HALF], F32)
            sbf("wbc0", [128, T_HALF], F32)
            t["wbc"] = [t["wbc0"]]
            sbf("galla_b", [128, KT * 128], BF16)
            sbf("t1r_e", [16, T_HALF], BF16)
            sbf("t1b", [128, 512], BF16)
            sbf("gupB_b", [16, D], BF16)
            sbf("sil_s", [128, 512], F32)
            sbf("silq", [128, T_HALF], BF16)
            sbf("htf", [128, 512], F32)
            sbf("ht_b", [128, 512], BF16)

        mctx.close()

        with ExitStack() as bctx:
            def sbb(n, s, dt=F32):
                t[n] = bctx.enter_context(nc.sbuf_tensor(n, s, dt))

            sbb("yacc", [128, KT * T_HALF], F32)
            sbb("hload", [128, 8 * T_HALF], BF16)
            sbb("dwAf", [128, 8 * R_LORA], F32)
            sbb("dwnA_b", [128, 8 * R_LORA], BF16)
            sbb("dwnB_b", [16, D], BF16)
            sbb("t3r", [16, T_HALF], BF16)

        pst("PR", [128, 1024])
        pst("PO", [128, 1024])
        pst("PB", [128, 1024])
        pst("PC", [128, 512])
        pst("PX", [128, 512])

        sem_names = ["sx", "sp", "pe", "ac", "dv", "po", "so",
                     "cp0", "cp1", "ca0", "ca1", "cv0", "cv1"]
        sems = {n: ctx.enter_context(nc.semaphore(n)) for n in sem_names}

        block = ctx.enter_context(nc.Block())

        def mkpass(ename):
            def f(eng):
                ctrs = {n: Ctr(s) for n, s in sems.items()}
                Prog(ename, eng, t, ctrs, taps).run()
            return f

        block.sync(mkpass("sync"))
        block.tensor(mkpass("tensor"))
        block.scalar(mkpass("scalar"))
        block.vector(mkpass("vector"))
        block.gpsimd(mkpass("gpsimd"))

    return nc


def _tile_cols(w, kt, ot):
    K, O = w.shape
    r = w.reshape(kt, 128, ot, 128)
    return np.ascontiguousarray(r.transpose(2, 1, 0, 3).reshape(ot, 128, kt * 128))


def stage_inputs(inputs):
    f = lambda k: np.asarray(inputs[k], np.float32)
    stat = {}
    stat["wq_t"] = _tile_cols(f("L0_attn_Wq"), KT, KT)
    stat["wk_t"] = _tile_cols(f("L0_attn_Wk"), KT, KV_OT)
    stat["wv_t"] = _tile_cols(f("L0_attn_Wv"), KT, KV_OT)
    stat["wg_t"] = _tile_cols(f("L0_attn_Wg"), KT, KT)
    stat["wo_t"] = _tile_cols(f("L0_attn_Wo"), KT, KT)
    wsm = np.zeros((D, 128), np.float32)
    wsm[:, 0:16] = f("L0_attn_Wb")
    wsm[:, 32:48] = f("L0_attn_Wa")
    wsm[:, 64:80] = f("L0_attn_Walpha")
    stat["wsm_t"] = _tile_cols(wsm, KT, 1)
    def _conv_lay(w, ot):
        return np.ascontiguousarray(w.reshape(ot, 128, CONV).transpose(1, 0, 2).reshape(128, ot * CONV))
    stat["convq"] = _conv_lay(f("L0_attn_conv_q"), KT)
    stat["convk"] = _conv_lay(f("L0_attn_conv_k"), KV_OT)
    stat["convv"] = _conv_lay(f("L0_attn_conv_v"), KV_OT)
    sv = np.zeros((128, 2), np.float32)
    sv[0:16, 0] = f("L0_attn_bb")
    sv[32:48, 0] = f("L0_attn_dt_bias")
    sv[64:80, 0] = f("L0_attn_balpha")
    sv[32:48, 1] = f("L0_attn_A_log")
    stat["svec"] = sv
    stat["nwv"] = f("L0_attn_norm_w").reshape(128, 1)
    gup = f("L0_ffn_gate_up_proj")
    stat["gup_t"] = np.stack([_tile_cols(gup[e], KT, KT) for e in range(E_EXP)])
    dwn = f("L0_ffn_down_proj")
    stat["dwn_t"] = np.stack([_tile_cols(dwn[e], 8, KT) for e in range(E_EXP)])
    stat["gw_tl"] = np.ascontiguousarray(
        f("L0_ffn_gate_weight").T.reshape(KT, 128, E_EXP).transpose(1, 0, 2).reshape(128, KT * E_EXP))
    stat["ga_tl"] = f("L0_ffn_gate_lora_A")
    stat["gb_tl"] = np.ascontiguousarray(f("L0_ffn_gate_lora_B").T)
    gla = f("L0_ffn_gup_lora_A")
    stat["gupA_tl"] = np.ascontiguousarray(
        gla.transpose(2, 0, 1).reshape(KT, 128, E_EXP * R_LORA)
        .transpose(1, 0, 2).reshape(128, KT * 128))
    stat["gupB_tl"] = np.ascontiguousarray(
        f("L0_ffn_gup_lora_B").transpose(0, 2, 1))
    dla = f("L0_ffn_down_lora_A")
    stat["dwnA_tl"] = np.ascontiguousarray(
        dla.transpose(0, 2, 1).reshape(E_EXP, 8, 128, R_LORA)
        .transpose(0, 2, 1, 3).reshape(E_EXP, 128, 8 * R_LORA))
    stat["dwnB_tl"] = np.ascontiguousarray(
        f("L0_ffn_down_lora_B").transpose(0, 2, 1))
    x = f("L0_attn_hidden_states")
    in_maps = []
    for core in range(8):
        b = core // 2
        hi = core % 2
        im = dict(stat)
        xb = x[b]
        if hi == 0:
            halo = np.zeros((3, D), np.float32)
            xs = np.concatenate([halo, xb[0:T_HALF]], 0)
            xp = np.zeros((3 + T_HALF, D), np.float32)
            im["pmask"] = np.zeros((128, 1), np.float32)
        else:
            xs = np.concatenate([xb[T_HALF - 3:T_HALF], xb[T_HALF:]], 0)
            xp = np.concatenate([np.zeros((3, D), np.float32), xb[0:T_HALF]], 0)
            im["pmask"] = np.ones((128, 1), np.float32)
        im["x_t"] = np.ascontiguousarray(xs.T)
        im["xp_t"] = np.ascontiguousarray(xp.T)
        in_maps.append(im)
    return in_maps


def kernel(**inputs):
    in_maps = stage_inputs(inputs)
    nc = build_nc(DEBUG_TAPS)
    res = run_bass_kernel_spmd(nc, in_maps, core_ids=list(range(8)))
    out = np.empty((4, 2 * T_HALF, D), np.float32)
    for core in range(8):
        b, hi = core // 2, core % 2
        out[b, hi * T_HALF:(hi + 1) * T_HALF, :] = res.results[core]["out_t"].T
    if DEBUG_TAPS:
        kernel.last_results = res
    return out
